# version 10 (final)
# speedup vs baseline: 1.1853x; 1.0068x over previous
import sys
if '/opt/trn_rl_repo' not in sys.path:
    sys.path.insert(0, '/opt/trn_rl_repo')
import contextlib
import numpy as np
import concourse.bass as bass
import concourse.mybir as mybir
from concourse.bass_utils import run_bass_kernel_spmd

F32 = mybir.dt.float32
BF16 = mybir.dt.bfloat16
AF = mybir.ActivationFunctionType
ALU = mybir.AluOpType
AX = mybir.AxisListType

B, T, C, H, HD = 16, 2048, 1024, 16, 64
NCH = C // 128
NCORE = 8
BPC = B // NCORE
NPASS = 4
HPP = H // NPASS
EPS_LN = 1e-5
EPS_GN = 1e-5 * 64
SOFT_SHIFT = 20.0

STAGES = 4

B_SKIP = set()
B_C2N = 8
B_C2OFF = 0
B_TCN = 8
ENGS = ("sync", "gpsimd", "tensor", "vector", "scalar")


class Sem:
    def __init__(self, h):
        self.h = h
        self.n = 0

    def bump(self, k):
        self.n += k
        return self.n


class Prog:
    def __init__(self):
        self.ops = {e: [] for e in ENGS}

    def op(self, eng, fn, waits=(), incs=(), dr=None):
        planned = [(s, s.bump(k)) for (s, k) in incs]
        steps = tuple(incs)
        if dr is None:
            dr = eng in ("vector", "scalar", "gpsimd")

        def run(e, fn=fn, waits=tuple(waits), steps=steps, dr=dr):
            if dr:
                e.drain()
            for (s, v) in waits:
                e.wait_ge(s.h, v)
            ins = fn(e)
            for (s, k) in steps:
                ins.then_inc(s.h, k)
        self.ops[eng].append(run)
        return planned[0][1] if planned else None

    def wait(self, eng, waits):
        def run(e, waits=tuple(waits)):
            for (s, v) in waits:
                e.wait_ge(s.h, v)
        self.ops[eng].append(run)

    def finalize(self, block):
        for eng in ENGS:
            oplist = self.ops[eng]

            def body(e, oplist=oplist):
                for fn in oplist:
                    fn(e)
            getattr(block, eng)(body)


def build_host_inputs(inp):
    g = {}
    x = np.asarray(inp["L0_attn_a_x"], dtype=np.float32)
    g["xF"] = np.ascontiguousarray(x.transpose(0, 2, 1).reshape(B, NCH, 128, T))

    def wT(w):
        return np.ascontiguousarray(np.asarray(w, dtype=np.float32).T)

    for name, key in (("wq", "L0_attn_a_Wq"), ("wk", "L0_attn_a_Wk"), ("wv", "L0_attn_a_Wv")):
        wt = wT(inp[key]).reshape(NCH, 128, NPASS, HPP * HD)
        g[name] = np.ascontiguousarray(wt.transpose(2, 0, 1, 3))
    g["wo_a"] = np.ascontiguousarray(wT(inp["L0_attn_a_Wo"]).reshape(NCH, 128, C))
    g["wff"] = np.ascontiguousarray(wT(inp["L0_ffn_W"]).reshape(NCH, 128, C))
    g["ld_T"] = np.ascontiguousarray(np.asarray(inp["L0_ffn_lora_down"], np.float32).T)
    g["lu_T"] = np.ascontiguousarray(np.asarray(inp["L0_ffn_lora_up"], np.float32).T)
    g["da"] = np.ascontiguousarray(np.asarray(inp["L0_ffn_down_aux"], np.float32))
    g["ua_T"] = np.ascontiguousarray(np.asarray(inp["L0_ffn_up_aux"], np.float32).T)
    g["bias_ff"] = np.ascontiguousarray(np.asarray(inp["L0_ffn_b"], np.float32).reshape(NCH, 128).T)

    for name, key in (("wr", "L0_attn_b_W_r"), ("wkb", "L0_attn_b_W_k"),
                      ("wvb", "L0_attn_b_W_v"), ("wg", "L0_attn_b_W_g"),
                      ("wo_b", "L0_attn_b_W_o")):
        g[name] = np.ascontiguousarray(np.asarray(inp[key], np.float32).reshape(NCH, 128, C))
    w1 = np.asarray(inp["L0_attn_b_time_maa_w1"], np.float32)
    w1A = np.concatenate([w1[:, 0:32], w1[:, 32:64], w1[:, 96:128]], axis=1)
    w1B = np.concatenate([w1[:, 64:96], w1[:, 128:160]], axis=1)
    g["w1a"] = np.ascontiguousarray(w1A.reshape(NCH, 128, 96))
    g["w1b"] = np.ascontiguousarray(w1B.reshape(NCH, 128, 64))
    w2_ = np.asarray(inp["L0_attn_b_time_maa_w2"], np.float32)
    g["w2"] = np.ascontiguousarray(w2_[[0, 1, 3, 2, 4]])
    g["tdw1"] = np.ascontiguousarray(np.asarray(inp["L0_attn_b_time_decay_w1"], np.float32).reshape(NCH, 128, 64))
    g["tdw2"] = np.ascontiguousarray(np.asarray(inp["L0_attn_b_time_decay_w2"], np.float32))

    def fcol(v):
        return np.ascontiguousarray(np.asarray(v, np.float32).reshape(-1)[:C].reshape(NCH, 128).T)

    g["tmx"] = fcol(inp["L0_attn_b_time_maa_x"])
    tm5 = np.stack([fcol(inp["L0_attn_b_time_maa_w"]), fcol(inp["L0_attn_b_time_maa_k"]),
                    fcol(inp["L0_attn_b_time_maa_r"]), fcol(inp["L0_attn_b_time_maa_v"]),
                    fcol(inp["L0_attn_b_time_maa_g"])], axis=2)
    g["tm5"] = np.ascontiguousarray(tm5.reshape(128, 40))
    g["tdecay"] = fcol(inp["L0_attn_b_time_decay"])
    g["ucols"] = fcol(np.asarray(inp["L0_attn_b_time_faaaa"], np.float32).reshape(-1))
    g["lnw"] = np.ascontiguousarray(np.broadcast_to(
        np.asarray(inp["L0_attn_b_ln_w"], np.float32), (128, C)))
    g["lnb"] = np.ascontiguousarray(np.broadcast_to(
        np.asarray(inp["L0_attn_b_ln_b"], np.float32), (128, C)))

    g["ones_row"] = np.ones((1, 128), np.float32)
    o2 = np.zeros((128, 2), np.float32)
    o2[0:64, 0] = 1.0
    o2[64:128, 1] = 1.0
    g["ones2"] = o2
    g["ident"] = np.eye(128, dtype=np.float32)
    perm = np.zeros((128, 128), np.float32)
    for i in range(64):
        perm[2 * i + 1, 2 * i] = -1.0
        perm[2 * i, 2 * i + 1] = 1.0
    g["permT"] = perm
    iv = (10000.0 ** (-np.arange(0, HD, 2, dtype=np.float64) / HD))
    pos = np.arange(T, dtype=np.float64)
    ang = pos[None, :] * iv[(np.arange(HD) % 32)][:, None]
    cos64 = np.cos(ang).astype(np.float32)
    sin64 = np.sin(ang).astype(np.float32)
    g["cosT"] = np.ascontiguousarray(np.vstack([cos64, cos64]))
    g["sinT"] = np.ascontiguousarray(np.vstack([sin64, sin64]))
    dm = np.zeros((4, 128, 512), np.float32)
    for j in range(4):
        for tk in range(128):
            lo = 128 * j + tk
            if lo < 512:
                dm[j, tk, lo:] = 1.0
    g["diagmask"] = np.ascontiguousarray(dm.reshape(4 * 128, 512))
    sl = np.zeros((128, 128), np.float32)
    for s_ in range(128):
        sl[s_, s_ + 1:] = 1.0
    g["slmask"] = sl
    return g


def build_nc():
    nc = bass.Bass("TRN2", target_bir_lowering=False)
    dram = {}

    def din(name, shape):
        dram[name] = nc.dram_tensor(name, list(shape), F32, kind="ExternalInput").ap()

    din("xF", (BPC, NCH, 128, T))
    for n in ("wq", "wk", "wv"):
        din(n, (NPASS, NCH, 128, HPP * HD))
    din("wo_a", (NCH, 128, C))
    din("wff", (NCH, 128, C))
    din("ld_T", (100, 4)); din("lu_T", (4, 50)); din("da", (100, C)); din("ua_T", (50, C))
    din("bias_ff", (128, NCH))
    for n in ("wr", "wkb", "wvb", "wg", "wo_b"):
        din(n, (NCH, 128, C))
    din("w1a", (NCH, 128, 96)); din("w1b", (NCH, 128, 64))
    din("w2", (5, 32, C))
    din("tdw1", (NCH, 128, 64)); din("tdw2", (64, C))
    din("tmx", (128, NCH)); din("tm5", (128, 40)); din("tdecay", (128, NCH)); din("ucols", (128, NCH))
    din("lnw", (128, C)); din("lnb", (128, C))
    din("ones_row", (1, 128)); din("ident", (128, 128)); din("ones2", (128, 2))
    din("permT", (128, 128)); din("cosT", (128, T)); din("sinT", (128, T))
    din("diagmask", (4 * 128, 512)); din("slmask", (128, 128))
    yout = nc.dram_tensor("yout", [BPC, NCH, 128, T], F32, kind="ExternalOutput").ap()

    with contextlib.ExitStack() as stk:
        def sb(name, shape, dt):
            return stk.enter_context(nc.sbuf_tensor(name, list(shape), dt))

        xA = sb("xA", (128, NCH * T), F32)
        A1 = sb("A1", (128, 16384), BF16)
        A2 = sb("A2", (128, 16384), BF16)
        A3 = sb("A3", (128, 10496), BF16)
        A4 = sb("A4", (128, 2440), F32)
        WB = sb("WB", (128, 16384), BF16)
        PSC = sb("PSC", (128, 2048), F32)
        ROWS = sb("ROWS", (1, 512), F32)
        CST = sb("CST", (128, 160), F32)
        CSTB = sb("CSTB", (128, 392), BF16)
        SST = sb("SST", (128, 512), F32)
        SSTB = sb("SSTB", (128, 512), BF16)
        TW = sb("TW", (64, 1024), BF16)
        ONE1 = sb("ONE1", (1, 128), F32)
        A1f = A1[:, :].bitcast(F32)
        DAUX = A1f[0:100, 0:1024]
        UAUX = A1f[0:50, 1024:2048]
        T1 = A1f[0:4, 2048:3072]
        T2 = A1f[0:50, 3072:4096]
        LORA = A1f[0:100, 4096:4160]

        sems = {n: Sem(stk.enter_context(nc.semaphore(n))) for n in
                ("sL", "sW", "sWa", "sWb", "sK", "sPE", "sDV", "sAC", "sGP", "sSY", "sST")}
        sL, sW = sems["sL"], sems["sW"]
        sWa, sWb, sK = sems["sWa"], sems["sWb"], sems["sK"]
        sPE, sDV, sAC = sems["sPE"], sems["sDV"], sems["sAC"]
        sGP, sSY, sST = sems["sGP"], sems["sSY"], sems["sST"]
        esem = {"sync": sSY, "gpsimd": sGP, "tensor": sPE, "vector": sDV, "scalar": sAC}

        p = Prog()

        def barrier(dma=False):
            vals = {}
            for eng in ENGS:
                s = esem[eng]
                vals[eng] = s.bump(1)

                def run(e, s=s):
                    e.drain()
                    e.sem_inc(s.h, 1)
                p.ops[eng].append(run)
            extra = ([(sm, sm.n) for sm in (sL, sW, sWa, sWb, sK) if sm.n > 0]
                     if dma else [])
            for eng in ENGS:
                p.wait(eng, [(esem[o], vals[o]) for o in ENGS if o != eng] + extra)

        def xa(b_, c_):
            return xA[:, c_ * T:(c_ + 1) * T]

        cosV = A4[:, 0:512].bitcast(BF16)
        cosV = A4[:, 0:1024].bitcast(BF16)
        sinV = A4[:, 1024:2048].bitcast(BF16)

        def tmx_col(c_):
            return CST[:, c_:c_ + 1]

        def tm5_cols(c_, s0, ns):
            return CST[:, 8 + c_ * 5 + s0: 8 + c_ * 5 + s0 + ns]

        td_col = lambda c_: CST[:, 48 + c_:49 + c_]
        u_col = lambda c_: CST[:, 56 + c_:57 + c_]
        bias_col = lambda c_: CST[:, 64 + c_:65 + c_]
        c_shift = CST[:, 72:73]
        c_epsln = CST[:, 74:75]
        c_epsgn = CST[:, 75:76]
        onecol = CST[:, 76:77]
        lnwB = WB[:, 13312:14336]
        lnbB = WB[:, 14336:15360]
        identV = CSTB[:, 0:128]
        ones2V = CSTB[:, 384:386]
        onecolB = CSTB[:, 386:387]
        permV = CSTB[:, 128:256]
        slmaskV = CSTB[:, 256:384]

        def ld(q, dst, src, sem=None, waits=()):
            s = sem if sem is not None else (sL if q == "sync" else sW)
            return p.op(q, lambda e, dst=dst, src=src: e.dma_start(out=dst, in_=src),
                        waits=waits, incs=[(s, 16)])

        ld("sync", CST[:, 0:8], dram["tmx"][:, :])
        ld("sync", CST[:, 8:48], dram["tm5"][:, :])
        ld("sync", CST[:, 48:56], dram["tdecay"][:, :])
        ld("sync", CST[:, 56:64], dram["ucols"][:, :])
        ld("sync", CST[:, 64:72], dram["bias_ff"][:, :])
        ld("sync", ONE1[:, :], dram["ones_row"][:, :])
        ld("gpsimd", identV, dram["ident"][:, :])
        ld("gpsimd", permV, dram["permT"][:, :])
        ld("gpsimd", slmaskV, dram["slmask"][:, :])
        ld("gpsimd", ones2V, dram["ones2"][:, :])
        x_loads = {}
        x_ready = {}
        p.op("vector", lambda e: e.memset(CST[:, 72:73], -SOFT_SHIFT), incs=[(sDV, 1)])
        p.op("vector", lambda e: e.memset(CST[:, 74:75], EPS_LN))
        p.op("vector", lambda e: e.memset(CST[:, 75:76], EPS_GN))
        p.op("vector", lambda e: e.memset(CST[:, 76:77], 1.0))
        p.op("vector", lambda e: e.memset(onecolB, 1.0), incs=[(sDV, 1)])
        barrier(dma=True)

        psA_ = stk.enter_context(nc.psum_tensor("psA", [128, 512], F32))
        psB_ = stk.enter_context(nc.psum_tensor("psB", [128, 512], F32))
        psG_ = stk.enter_context(nc.psum_tensor("psG", [128, 1024], F32))
        psC_ = psG_[:, 0:512]
        psD_ = psG_[:, 512:1024]
        psE_ = stk.enter_context(nc.psum_tensor("psE", [128, 1024], F32))
        psF_ = stk.enter_context(nc.psum_tensor("psF", [128, 1024], F32))

        def ln_phase(b_, dst):
            r_m = ROWS[0:1, 0:512]
            r_rs = PSC[0:1, 1536:2048]
            sq = (PSC[:, 0:512].bitcast(BF16)[:, 0:512], PSC[:, 512:1024].bitcast(BF16)[:, 0:512])
            pe_sq_consumed = [0, 0]
            dv_rows = 0
            dv_apply = 0
            for tb in range(4):
                t0 = tb * 512
                ac_sq = [0] * NCH
                for c_ in range(NCH):
                    sqb = sq[c_ % 2]
                    ac_sq[c_] = p.op("scalar",
                                     lambda e, sqb=sqb, src=xa(b_, c_)[:, t0:t0 + 512]:
                                     e.activation(sqb, src, AF.Square),
                                     waits=[(sPE, pe_sq_consumed[c_ % 2]),
                                            (sL, x_ready.get(b_, 0))], incs=[(sAC, 1)])
                    p.op("tensor", lambda e, src=xa(b_, c_)[:, t0:t0 + 512], c_=c_:
                         e.matmul(psA_[0:1, :], onecol, src,
                                  start=(c_ == 0), stop=(c_ == NCH - 1)),
                         waits=([(sL, x_ready.get(b_, 0)), (sDV, dv_rows)] if c_ == 0 else []),
                         incs=[(sPE, 1)])
                    pe_sq_consumed[c_ % 2] = p.op("tensor",
                                                  lambda e, sqb=sqb, c_=c_:
                                                  e.matmul(psB_[0:1, :], onecolB, sqb,
                                                           start=(c_ == 0), stop=(c_ == NCH - 1)),
                                                  waits=[(sAC, ac_sq[c_])], incs=[(sPE, 1)])
                pe_stats = sPE.n
                p.op("vector", lambda e: e.tensor_scalar_mul(r_m, psA_[0:1, :], 1.0 / C),
                     waits=[(sPE, pe_stats)])
                p.op("vector", lambda e: e.tensor_tensor(r_rs, r_m, r_m, ALU.mult))
                dv_rows = p.op("vector", lambda e: e.scalar_tensor_tensor(
                    r_rs, psB_[0:1, :], 1.0 / C, r_rs, ALU.mult, ALU.subtract),
                    incs=[(sDV, 1)])
                p.op("scalar", lambda e: e.activation(r_rs, r_rs, AF.Ln, bias=c_epsln[0:1, :]),
                     waits=[(sDV, dv_rows)])
                ac_r = p.op("scalar", lambda e: e.activation(r_rs, r_rs, AF.Exp, scale=-0.5),
                            incs=[(sAC, 1)])
                p.op("tensor", lambda e: e.matmul(psC_[:, :], ONE1[:, :], r_rs, start=True, stop=True),
                     waits=[(sAC, ac_r), (sDV, dv_apply)], incs=[(sPE, 1)])
                dv_mr = p.op("vector", lambda e: e.tensor_tensor(r_m, r_m, r_rs, ALU.mult),
                             waits=[(sAC, ac_r)], incs=[(sDV, 1)])
                pe_bc = p.op("tensor", lambda e: e.matmul(psD_[:, :], ONE1[:, :], r_m, start=True, stop=True),
                             waits=[(sDV, dv_mr)], incs=[(sPE, 1)])
                tmp = PSC[:, 1024:1536]
                for c_ in range(NCH):
                    p.op("vector", lambda e, src=xa(b_, c_)[:, t0:t0 + 512], tmp=tmp:
                         e.tensor_tensor(tmp, src, psC_[:, :], ALU.mult),
                         waits=([(sPE, pe_bc)] if c_ == 0 else []))
                    dv_apply = p.op("vector", lambda e, tmp=tmp, d=dst(c_, tb):
                                    e.tensor_tensor(d, tmp, psD_[:, :], ALU.subtract),
                                    incs=[(sDV, 1)])
            barrier()

        def hA(c_, tb=None):
            v = A1[:, c_ * T:(c_ + 1) * T]
            return v if tb is None else v[:, tb * 512:(tb + 1) * 512]

        def attn_a(b_):
            ln_phase(b_, lambda c_, tb: hA(c_, tb))
            ld("gpsimd", cosV, dram["cosT"][:, :], sem=sK)
            ld("gpsimd", sinV, dram["sinT"][:, :], sem=sK)
            dm_ld = ld("gpsimd",
                       PSC[:, 1024:2048].bitcast(BF16).rearrange("p (j n) -> p j n", j=4),
                       dram["diagmask"][:, :].rearrange("(j p) n -> p j n", p=128), sem=sK)
            cs_ld = dm_ld

            def dmask(j):
                return PSC[:, 1024 + j * 256:1024 + (j + 1) * 256].bitcast(BF16)

            def pbuf(h2, par):
                i = h2 * 2 + par
                return PSC[:, i * 256:(i + 1) * 256].bitcast(BF16)

            Qpr = A3[:, 0:T]
            Kpr = A3[:, T:2 * T]

            def V65(tc):
                return A3[:, 2 * T + tc * 260: 2 * T + tc * 260 + 260]

            rbA3 = A3[:, 8256:9280].bitcast(F32)

            wfree_pass = {}
            pe_attn_global = [0]
            dv_attn_global = [0]
            ac_attn_global = [0]
            dv_ones = 0
            for tc in range(16):
                dv_ones = p.op("vector", lambda e, ap=V65(tc)[:, 64:260:65]: e.memset(ap, 1.0),
                               incs=[(sDV, 1)])

            def wv_view(ps_, wn):
                wi = ("wq", "wk", "wv").index(wn)
                return WB[:, (ps_ % 2) * 8192 + wi * 2048:(ps_ % 2) * 8192 + (wi + 1) * 2048]

            def load_pass_weights(ps_, slot_free_pe):
                sem = (sWa, sWb)[ps_ % 2]
                vals = []
                for wn in ("wq", "wk", "wv"):
                    dst = wv_view(ps_, wn).rearrange("p (c n) -> p c n", c=NCH)
                    src = dram[wn][ps_, :, :, :].rearrange("c p n -> p c n")
                    vals.append(ld("gpsimd", dst, src, sem=sem,
                                   waits=([(sPE, slot_free_pe)] if wn == "wq" else [])))
                return max(vals)

            wready = {0: load_pass_weights(0, wfree_pass.get(0, 0))}
            for ps_ in range(NPASS):
                if ps_ + 1 < NPASS:
                    wready[ps_ + 1] = load_pass_weights(ps_ + 1, wfree_pass.get(ps_ + 1, 0))
                wsem = (sWa, sWb)[ps_ % 2]
                wq_v, wk_v, wv_v = (wv_view(ps_, n) for n in ("wq", "wk", "wv"))
                pe_attn_prev = pe_attn_global[0]
                dv_v = 0
                ac_guard = 0
                first_v = True
                for tc in range(16):
                    for c_ in range(NCH):
                        w_ = []
                        if c_ == 0:
                            w_ = [(sDV, dv_v)]
                            if first_v:
                                first_v = False
                                w_ = [(wsem, wready[ps_]), (sDV, dv_ones),
                                      (sDV, dv_attn_global[0])]
                        p.op("tensor", lambda e, c_=c_, tc=tc, wv_=wv_v:
                             e.matmul(psE_[:, 0:256],
                                      hA(c_)[:, tc * 128:tc * 128 + 128],
                                      wv_[:, c_ * 256:(c_ + 1) * 256],
                                      start=(c_ == 0), stop=(c_ == NCH - 1)),
                             waits=w_, incs=[(sPE, 1)])
                    pe_v = sPE.n
                    dv_v = p.op("vector", lambda e, tc=tc:
                                e.tensor_copy(
                                    V65(tc)[:, 0:260].rearrange("p (h n) -> p h n", h=4)[:, :, 0:64],
                                    psE_[:, 0:256].rearrange("p (h n) -> p h n", h=4)),
                                waits=[(sPE, pe_v)], incs=[(sDV, 1)], dr=(tc == 0))
                p.wait("tensor", [(sDV, dv_v)])
                for pr in range(2):
                    ac_qraw = {0: ac_attn_global[0], 1: ac_attn_global[0]}
                    pe_rot = {0: 0, 1: 0}
                    dv_rope = {0: 0, 1: 0}
                    dv_qk_first = [True]
                    p.wait("vector", [(sPE, pe_attn_prev)])
                    p.wait("scalar", [(sPE, pe_attn_prev)])
                    ti = 0
                    pendq = None
                    tiles = [(arr, tb) for (arr, wsl_) in ((Qpr, wq_v), (Kpr, wk_v))
                             for tb in range(4)]

                    def emit_rope(item):
                        arr_, t0, par, qraw, qc, rs = item
                        pe_rot[par] = p.op("tensor",
                                           lambda e, qraw=qraw, par=par:
                                           e.matmul(psF_[:, par * 512:par * 512 + 512], permV, qraw,
                                                    start=True, stop=True),
                                           waits=[(sAC, ac_qraw[par]), (sDV, dv_rope[par])],
                                           incs=[(sPE, 1)])
                        p.op("vector", lambda e, qraw=qraw, qc=qc, t0=t0:
                             e.tensor_tensor(qc, qraw, cosV[:, t0:t0 + 512], ALU.mult),
                             waits=[(sAC, ac_qraw[par]), (sK, cs_ld)], dr=False)
                        p.op("vector", lambda e, rs=rs, t0=t0, par=par:
                             e.tensor_tensor(rs, psF_[:, par * 512:par * 512 + 512],
                                             sinV[:, t0:t0 + 512], ALU.mult),
                             waits=[(sPE, pe_rot[par])], dr=False)
                        dv_rope[par] = p.op("vector",
                                            lambda e, d=arr_[:, t0:t0 + 512], qc=qc, rs=rs:
                                            e.tensor_tensor(d, qc, rs, ALU.add),
                                            incs=[(sDV, 1)])

                    for (arr, tb) in tiles:
                        wsl = wq_v if arr is Qpr else wk_v
                        t0 = tb * 512
                        par = ti % 2
                        ti += 1
                        for c_ in range(NCH):
                            w_ = [(sAC, ac_qraw[par])] if c_ == 0 else []
                            if c_ == 0 and dv_qk_first[0]:
                                dv_qk_first[0] = False
                                w_ = w_ + [(sDV, dv_attn_global[0])]
                            p.op("tensor", lambda e, wsl=wsl, c_=c_, pr=pr, t0=t0, par=par:
                                 e.matmul(psE_[:, par * 512:par * 512 + 512],
                                          wsl[:, c_ * 256 + pr * 128: c_ * 256 + pr * 128 + 128],
                                          hA(c_)[:, t0:t0 + 512],
                                          start=(c_ == 0), stop=(c_ == NCH - 1)),
                                 waits=w_, incs=[(sPE, 1)])
                        pe_proj = sPE.n
                        qraw = PSC[:, par * 256:par * 256 + 256].bitcast(BF16)
                        qc = PSC[:, 512 + par * 256:768 + par * 256].bitcast(BF16)
                        rs = A3[:, 8256 + par * 512:8768 + par * 512]
                        if pendq is not None:
                            emit_rope(pendq)
                        ac_qraw[par] = p.op("scalar",
                                            lambda e, qraw=qraw, par=par:
                                            e.activation(qraw, psE_[:, par * 512:par * 512 + 512], AF.Copy),
                                            waits=[(sPE, pe_proj), (sDV, dv_rope[par])],
                                            incs=[(sAC, 1)], dr=(par == 0))
                        pendq = (arr, t0, par, qraw, qc, rs)
                    emit_rope(pendq)
                    dv_qk_done = max(dv_rope[0], dv_rope[1])
                    p.wait("tensor", [(sDV, dv_qk_done)])
                    p.wait("scalar", [(sDV, dv_qk_done)])
                    p.wait("vector", [(sDV, dv_qk_done)])
                    ac_exp = {0: 0, 1: 0}
                    pe_pv = {0: 0, 1: 0}
                    dv_norm = 0
                    ac_rb = 0
                    for tqb in range(4):
                        tq0 = tqb * 512
                        nchunks = (tq0 + 512) // 128
                        pend = None
                        for kc in range(nchunks + 1):
                            if kc < nchunks:
                                par = kc % 2
                                sb_ = psF_ if par == 0 else psG_
                                for h2 in range(2):
                                    rowo = 64 * h2
                                    w_ = [(sAC, ac_exp[par])] if h2 == 0 else []
                                    p.op("tensor",
                                         lambda e, kc=kc, rowo=rowo, tq0=tq0, sb_=sb_, h2=h2:
                                         e.matmul(sb_[:, h2 * 512:(h2 + 1) * 512],
                                                  Kpr[rowo:rowo + 64, kc * 128:(kc + 1) * 128],
                                                  Qpr[rowo:rowo + 64, tq0:tq0 + 512],
                                                  start=True, stop=True,
                                                  tile_position=(rowo, 0)),
                                         waits=w_, incs=[(sPE, 1)])
                                pe_sc = sPE.n
                            if pend is not None:
                                pkc, ppb2, pdvm = pend
                                for h2 in range(2):
                                    hh = pr * 2 + h2
                                    w_ = ([(sDV, pdvm)] if pdvm else
                                          [(sAC, ac_exp[pkc % 2])]) if h2 == 0 else []
                                    if pkc == 0 and h2 == 0:
                                        w_.append((sDV, dv_norm))
                                    pe_pv[pkc % 2] = p.op(
                                        "tensor",
                                        lambda e, hh=hh, pkc=pkc, ppb2=ppb2, h2=h2, nch=nchunks:
                                        e.matmul(psE_[0:65, h2 * 512:(h2 + 1) * 512],
                                                 V65(pkc)[:, hh * 65:hh * 65 + 65],
                                                 ppb2[:, h2 * 512:(h2 + 1) * 512],
                                                 start=(pkc == 0), stop=(pkc == nch - 1)),
                                        waits=w_, incs=[(sPE, 1)])
                                pend = None
                            if kc < nchunks:
                                pb2 = PSC[:, par * 512:(par + 1) * 512].bitcast(BF16)
                                ac_exp[par] = p.op(
                                    "scalar",
                                    lambda e, pb2=pb2, sb_=sb_:
                                    e.activation(pb2, sb_[:, 0:1024], AF.Exp,
                                                 bias=c_shift, scale=0.125),
                                    waits=[(sPE, pe_sc), (sPE, pe_pv[par])],
                                    incs=[(sAC, 1)])
                                diag = kc >= tq0 // 128
                                dvm = 0
                                if diag:
                                    j = kc - tq0 // 128
                                    dvm = p.op("vector",
                                               lambda e, pb2=pb2, j=j:
                                               e.tensor_tensor(
                                                   pb2.rearrange("p (h n) -> p h n", h=2),
                                                   pb2.rearrange("p (h n) -> p h n", h=2),
                                                   dmask(j).unsqueeze(1).broadcast_to([128, 2, 512]),
                                                   ALU.mult),
                                               waits=[(sAC, ac_exp[par]), (sK, dm_ld)],
                                               incs=[(sDV, 1)])
                                pend = (kc, pb2, dvm)
                        pe_all = sPE.n
                        rdn = (ROWS[0:1, 0:512], A3[0:1, 9280:10304].bitcast(F32))
                        acl = {}
                        pe2 = {}
                        for h2 in range(2):
                            acl[h2] = p.op("scalar", lambda e, h2=h2, r_=rdn[h2]:
                                           e.activation(r_, psE_[64:65, h2 * 512:(h2 + 1) * 512],
                                                        AF.Ln),
                                           waits=[(sPE, pe_all), (sPE, sPE.n)],
                                           incs=[(sAC, 1)], dr=(h2 == 0))
                        for h2 in range(2):
                            acl[h2] = p.op("scalar", lambda e, r_=rdn[h2]:
                                           e.activation(r_, r_, AF.Exp, scale=-1.0),
                                           incs=[(sAC, 1)], dr=(h2 == 0))
                        for h2 in range(2):
                            bA = psA_ if h2 == 0 else psB_
                            pe2[h2] = p.op("tensor", lambda e, bA=bA, r_=rdn[h2]:
                                           e.matmul(bA[0:64, :], ONE1[:, 0:64], r_,
                                                    start=True, stop=True),
                                           waits=[(sAC, acl[h2])], incs=[(sPE, 1)])
                        accp = {}
                        for h2 in range(2):
                            bA = psA_ if h2 == 0 else psB_
                            rb_ = rbA3[0:64, 0:512] if h2 == 0 else rbA3[64:128, 0:512]
                            accp[h2] = p.op("scalar", lambda e, bA=bA, rb_=rb_:
                                            e.activation(rb_, bA[0:64, :], AF.Copy),
                                            waits=[(sPE, pe2[h2]), (sDV, dv_norm)],
                                            incs=[(sAC, 1)], dr=(h2 == 0))
                        for h2 in range(2):
                            hglob = ps_ * 4 + pr * 2 + h2
                            rb_ = rbA3[0:64, 0:512] if h2 == 0 else rbA3[64:128, 0:512]
                            ytile = A2[:, (hglob // 2) * T + tq0:(hglob // 2) * T + tq0 + 512]
                            rowo = 64 * (hglob % 2)
                            dv_norm = p.op("vector",
                                           lambda e, ytile=ytile, rowo=rowo, h2=h2, rb_=rb_:
                                           e.tensor_tensor(ytile[rowo:rowo + 64, :],
                                                           psE_[0:64, h2 * 512:(h2 + 1) * 512],
                                                           rb_, ALU.mult),
                                           waits=[(sAC, accp[h2])], incs=[(sDV, 1)],
                                           dr=(h2 == 0))
                        ac_rb = max(accp[0], accp[1])
                    pe_attn_prev = sPE.n
                    pe_attn_global[0] = sPE.n
                    dv_attn_global[0] = dv_norm
                    ac_attn_global[0] = max(ac_exp[0], ac_exp[1], ac_rb)
                wfree_pass[ps_ + 2] = sPE.n
            for c_ in range(NCH):
                ld("gpsimd", WB[:, c_ * 1024:(c_ + 1) * 1024], dram["wo_a"][c_, :, :], sem=sK,
                   waits=[(sPE, pe_attn_global[0])] if c_ == 0 else [])
            wo_done = sK.n
            dv_bank = {0: 0, 1: 0}
            for oc in range(NCH):
                for tb in range(4):
                    t0 = tb * 512
                    bk = (oc * 4 + tb) % 2
                    ps = psE_ if bk == 0 else psF_
                    for c_ in range(NCH):
                        w_ = ([(sK, wo_done), (sDV, dv_attn_global[0]),
                               (sAC, ac_attn_global[0])]
                              if (oc == 0 and tb == 0 and c_ == 0) else [])
                        if c_ == 0:
                            w_ = w_ + [(sDV, dv_bank[bk])]
                        p.op("tensor", lambda e, c_=c_, oc=oc, t0=t0, ps=ps:
                             e.matmul(ps[:, 0:512],
                                      WB[:, c_ * 1024 + oc * 128: c_ * 1024 + oc * 128 + 128],
                                      A2[:, c_ * T + t0:c_ * T + t0 + 512],
                                      start=(c_ == 0), stop=(c_ == NCH - 1)),
                             waits=w_, incs=[(sPE, 1)])
                    pe1 = sPE.n
                    dv_bank[bk] = p.op("vector", lambda e, oc=oc, t0=t0, ps=ps:
                                       e.tensor_tensor(xa(b_, oc)[:, t0:t0 + 512],
                                                       xa(b_, oc)[:, t0:t0 + 512],
                                                       ps[:, 0:512], ALU.add),
                                       waits=[(sPE, pe1)], incs=[(sDV, 1)])
            barrier()

        def build_weff():
            dv = 0
            l1 = ld("sync", LORA[:, 0:4], dram["ld_T"][:, :])
            l2 = ld("sync", LORA[0:4, 4:54], dram["lu_T"][:, :])
            l3 = ld("sync", DAUX[:, :], dram["da"][:, :])
            l4 = ld("sync", UAUX[:, :], dram["ua_T"][:, :])
            p.wait("tensor", [(sL, max(l1, l2, l3, l4))])
            p.wait("sync", [(sL, max(l1, l2, l3, l4))])
            for q_ in range(2):
                pe1 = p.op("tensor", lambda e, q_=q_:
                           e.matmul(psA_[0:4, :], LORA[0:100, 0:4],
                                    DAUX[0:100, q_ * 512:(q_ + 1) * 512], start=True, stop=True),
                           waits=[(sDV, dv)], incs=[(sPE, 1)])
                dv = p.op("vector", lambda e, q_=q_:
                          e.tensor_copy(T1[0:4, q_ * 512:(q_ + 1) * 512], psA_[0:4, :]),
                          waits=[(sPE, pe1)], incs=[(sDV, 1)])
            for q_ in range(2):
                pe1 = p.op("tensor", lambda e, q_=q_:
                           e.matmul(psA_[0:50, :], LORA[0:4, 4:54],
                                    T1[0:4, q_ * 512:(q_ + 1) * 512], start=True, stop=True),
                           waits=[(sDV, dv)], incs=[(sPE, 1)])
                dv = p.op("vector", lambda e, q_=q_:
                          e.tensor_copy(T2[0:50, q_ * 512:(q_ + 1) * 512], psA_[0:50, :]),
                          waits=[(sPE, pe1)], incs=[(sDV, 1)])
            for c_ in range(NCH):
                lw = ld("sync", PSC[:, 0:1024], dram["wff"][c_, :, :])
                for q_ in range(2):
                    pe1 = p.op("tensor", lambda e, c_=c_, q_=q_:
                               e.matmul(psB_[:, :], T2[0:50, c_ * 128:(c_ + 1) * 128],
                                        UAUX[0:50, q_ * 512:(q_ + 1) * 512], start=True, stop=True),
                               waits=[(sDV, dv)], incs=[(sPE, 1)])
                    dv = p.op("vector", lambda e, c_=c_, q_=q_:
                              e.tensor_tensor(WB[:, c_ * 1024 + q_ * 512: c_ * 1024 + q_ * 512 + 512],
                                              PSC[:, q_ * 512:(q_ + 1) * 512], psB_[:, :], ALU.add),
                              waits=[(sPE, pe1), (sL, lw)], incs=[(sDV, 1)])
                p.wait("sync", [(sDV, dv)])
            barrier()

        def ffn(b_):
            build_weff()
            ln_phase(b_, lambda c_, tb: hA(c_, tb))
            dv_bank = {0: 0, 1: 0}
            for oc in range(NCH):
                for tb in range(4):
                    t0 = tb * 512
                    bk = (oc * 4 + tb) % 2
                    ps = psE_ if bk == 0 else psF_
                    for c_ in range(NCH):
                        w_ = [(sDV, dv_bank[bk])] if c_ == 0 else []
                        p.op("tensor", lambda e, c_=c_, oc=oc, t0=t0, ps=ps:
                             e.matmul(ps[:, 0:512],
                                      WB[:, c_ * 1024 + oc * 128: c_ * 1024 + oc * 128 + 128],
                                      hA(c_)[:, t0:t0 + 512],
                                      start=(c_ == 0), stop=(c_ == NCH - 1)),
                             waits=w_, incs=[(sPE, 1)])
                    pe1 = sPE.n
                    tmp = PSC[:, 1536:2048]
                    p.op("vector", lambda e, oc=oc, ps=ps, tmp=tmp:
                         e.tensor_scalar(tmp, ps[:, 0:512], bias_col(oc), None, ALU.add),
                         waits=[(sPE, pe1)])
                    dv_bank[bk] = p.op("vector", lambda e, oc=oc, t0=t0, tmp=tmp:
                                       e.tensor_tensor(xa(b_, oc)[:, t0:t0 + 512],
                                                       xa(b_, oc)[:, t0:t0 + 512], tmp, ALU.add),
                                       incs=[(sDV, 1)])
            barrier()

        TB = 256
        TH = 1024

        def attn_b(b_):
            w1aV = lambda c_: A3[:, 9216 + c_ * 96: 9216 + (c_ + 1) * 96]
            tdw1V = lambda c_: A3[:, 9984 + c_ * 64: 9984 + (c_ + 1) * 64]
            w1bV = lambda c_: A3[:, 9216 + c_ * 64: 9216 + (c_ + 1) * 64]
            tdw2V = lambda f_: WB[0:64, 12288 + f_ * 128: 12288 + (f_ + 1) * 128]
            p.op("vector", lambda e: e.memset(SST[:, :], 0.0))
            p.op("vector", lambda e: e.memset(CST[:, 152:160], 1.0))
            dv0 = p.op("vector", lambda e: e.memset(SSTB[:, :], 0.0), incs=[(sDV, 1)])
            p.wait("tensor", [(sDV, dv0)])
            dv_state = {f: dv0 for f in range(8)}
            rF = lambda oc, t0, w: A1[:, oc * 1024 + t0: oc * 1024 + t0 + w]
            kF = lambda oc, t0, w: A1[:, 8192 + oc * 1024 + t0: 8192 + oc * 1024 + t0 + w]
            vT = lambda tc, o0, w: A2[:, tc * 1024 + o0: tc * 1024 + o0 + w]
            gT = lambda tc, o0, w: A2[:, 8192 + tc * 1024 + o0: 8192 + tc * 1024 + o0 + w]
            xxV = lambda c_: PSC[:, 0:1024].bitcast(BF16)[:, c_ * TB:(c_ + 1) * TB]
            hbV = lambda c_: PSC[:, 1024:2048].bitcast(BF16)[:, c_ * TB:(c_ + 1) * TB]
            t5s = [A4[:, 2056 + 128 * i:2184 + 128 * i].bitcast(BF16)[0:32, :] for i in range(3)]
            xxxb = A4[:, 2312:2440].bitcast(BF16)
            r_m = ROWS[0:1, 0:256]
            r_rs = ROWS[0:1, 256:512]

            for half in range(2):
                h0 = half * TH
                c1_wb_free = 0
                for pss in range(2):
                    ns = 3 if pss == 0 else 2
                    wsem = (sWa, sWb)[pss]
                    wnames = (("wr", "wkb"), ("wvb", "wg"))[pss]
                    wtot = 0
                    for wi, wn in enumerate(wnames):
                        wtot = ld("gpsimd",
                                  WB[:, wi * 8192:(wi + 1) * 8192].rearrange("p (c n) -> p c n", c=NCH),
                                  dram[wn][:, :, :].rearrange("c p n -> p c n"), sem=wsem,
                                  waits=[(sPE, c1_wb_free)] if wi == 0 else [])
                    wv0 = WB[:, 0:8192]
                    wv1 = WB[:, 8192:16384]
                    f0 = 0 if pss == 0 else 3
                    ld("gpsimd", A3[0:32, 6144:6144 + ns * 1024].rearrange("p (f n) -> p f n", f=ns),
                       dram["w2"][f0:f0 + ns, :, :].rearrange("f e n -> e f n"), sem=sK,
                       waits=[(sPE, c1_wb_free)])
                    if pss == 0:
                        ld("gpsimd", A3[:, 9216:9984].rearrange("p (c n) -> p c n", c=NCH),
                           dram["w1a"][:, :, :].rearrange("c p n -> p c n"), sem=sK)
                        w2tot = ld("gpsimd", A3[:, 9984:10496].rearrange("p (c n) -> p c n", c=NCH),
                                   dram["tdw1"][:, :, :].rearrange("c p n -> p c n"), sem=sK)
                    else:
                        w2tot = ld("gpsimd", A3[:, 9216:9728].rearrange("p (c n) -> p c n", c=NCH),
                                   dram["w1b"][:, :, :].rearrange("c p n -> p c n"), sem=sK)
                    smw_ready = w2tot
                    w2V = lambda j, oc: A3[0:32, 6144 + j * 1024 + oc * 128:
                                           6144 + j * 1024 + oc * 128 + 128]

                    def mixed(j, c_):
                        return A3[:, (j * NCH + c_) * TB:(j * NCH + c_ + 1) * TB]

                    def mixed_dst(c_):
                        return A3[:, 0:ns * NCH * TB].rearrange(
                            "p (j c n) -> p j c n", j=ns, c=NCH)[:, :, c_, :]

                    sqt = (PSC[:, 0:256].bitcast(BF16)[:, 0:256], PSC[:, 256:512].bitcast(BF16)[:, 0:256])
                    pe_sq = [0, 0]
                    dv_rows = 0
                    dv_happ = 0
                    ac_evac = 0
                    dv_mix = 0
                    dv_mix1 = {0: 0, 1: 0}
                    ac_proj = 0
                    ac_t5p = 0
                    for tb in range(4):
                        t0 = h0 + tb * TB
                        tl = tb * TB
                        ac_sq = 0
                        for c_ in range(NCH):
                            sqb = sqt[c_ % 2]
                            ac_sq = p.op("scalar", lambda e, sqb=sqb, src=xa(b_, c_)[:, t0:t0 + TB]:
                                         e.activation(sqb, src, AF.Square),
                                         waits=[(sPE, pe_sq[c_ % 2]), (sDV, dv_mix)],
                                         incs=[(sAC, 1)])
                            p.op("tensor", lambda e, src=xa(b_, c_)[:, t0:t0 + TB], c_=c_:
                                 e.matmul(psA_[0:1, 0:TB], onecol, src,
                                          start=(c_ == 0), stop=(c_ == NCH - 1)),
                                 waits=([(sDV, dv_rows)] if c_ == 0 else []), incs=[(sPE, 1)])
                            pe_sq[c_ % 2] = p.op("tensor", lambda e, sqb=sqb, c_=c_:
                                                 e.matmul(psB_[0:1, 0:TB], onecolB, sqb,
                                                          start=(c_ == 0), stop=(c_ == NCH - 1)),
                                                 waits=[(sAC, ac_sq)], incs=[(sPE, 1)])
                        pe_st = sPE.n
                        p.op("vector", lambda e: e.tensor_scalar_mul(r_m, psA_[0:1, 0:TB], 1.0 / C),
                             waits=[(sPE, pe_st)])
                        p.op("vector", lambda e: e.tensor_tensor(r_rs, r_m, r_m, ALU.mult))
                        dv_rows = p.op("vector", lambda e: e.scalar_tensor_tensor(
                            r_rs, psB_[0:1, 0:TB], 1.0 / C, r_rs, ALU.mult, ALU.subtract),
                            incs=[(sDV, 1)])
                        p.op("scalar", lambda e: e.activation(r_rs, r_rs, AF.Ln, bias=c_epsln[0:1, :]),
                             waits=[(sDV, dv_rows)])
                        ac_r = p.op("scalar", lambda e: e.activation(r_rs, r_rs, AF.Exp, scale=-0.5),
                                    incs=[(sAC, 1)])
                        p.op("tensor", lambda e: e.matmul(psC_[:, 0:TB], ONE1[:, :], r_rs,
                                                          start=True, stop=True),
                             waits=[(sAC, ac_r), (sDV, dv_happ), (sAC, ac_proj),
                                    (sDV, dv_mix)], incs=[(sPE, 1)])
                        dv_mr = p.op("vector", lambda e: e.tensor_tensor(r_m, r_m, r_rs, ALU.mult),
                                     waits=[(sAC, ac_r)], incs=[(sDV, 1)])
                        pe_bc = p.op("tensor", lambda e: e.matmul(psD_[:, 0:TB], ONE1[:, :], r_m,
                                                                  start=True, stop=True),
                                     waits=[(sDV, dv_mr), (sAC, ac_proj)], incs=[(sPE, 1)])
                        for c_ in range(NCH):
                            hj = A4[:, c_ * 257:(c_ + 1) * 257]
                            w_ = [(sPE, pe_bc), (sDV, dv_mix)] if c_ == 0 else []
                            if tb == 0 and half == 0:
                                p.op("vector", lambda e, hj=hj: e.memset(hj[:, 0:1], 0.0), waits=w_)
                            elif tb == 0:
                                p.op("vector", lambda e, hj=hj, c_=c_:
                                     e.tensor_copy(hj[:, 0:1], CST[:, 144 + c_:145 + c_]), waits=w_)
                            else:
                                p.op("vector", lambda e, hj=hj:
                                     e.tensor_copy(hj[:, 0:1], hj[:, 256:257]), waits=w_)
                            p.op("vector", lambda e, hj=hj, src=xa(b_, c_)[:, t0:t0 + TB]:
                                 e.tensor_tensor(hj[:, 1:257], src, psC_[:, 0:TB], ALU.mult))
                            p.op("vector", lambda e, hj=hj:
                                 e.tensor_tensor(hj[:, 1:257], hj[:, 1:257], psD_[:, 0:TB], ALU.subtract))
                            p.op("vector", lambda e, hj=hj, d=xxV(c_):
                                 e.tensor_tensor(d, hj[:, 0:256], hj[:, 1:257], ALU.subtract))
                            dv_happ = p.op("vector", lambda e, hj=hj, d=hbV(c_):
                                           e.tensor_copy(d, hj[:, 1:257]), incs=[(sDV, 1)])
                            if tb == 3 and pss == 0:
                                dv_happ = p.op("vector", lambda e, hj=hj, c_=c_: e.tensor_copy(
                                    CST[:, 144 + c_:145 + c_], hj[:, 256:257]), incs=[(sDV, 1)])
                        pe_w1 = 0
                        for c_ in range(NCH):
                            hj = A4[:, c_ * 257:(c_ + 1) * 257]
                            dv_x = p.op("vector", lambda e, c_=c_, hj=hj:
                                        e.scalar_tensor_tensor(xxxb, xxV(c_), tmx_col(c_),
                                                               hj[:, 1:257], ALU.mult, ALU.add),
                                        waits=[(sPE, pe_w1)], incs=[(sDV, 1)])
                            lhs1 = w1aV(c_)[:, 0:64] if pss == 0 else w1bV(c_)
                            p.op("tensor", lambda e, c_=c_, lhs1=lhs1: e.matmul(
                                psE_[0:64, 0:TB], lhs1, xxxb,
                                start=(c_ == 0), stop=(c_ == NCH - 1)),
                                waits=[(sDV, dv_x)] + ([(sK, smw_ready), (sAC, ac_t5p)] if c_ == 0 else []),
                                incs=[(sPE, 1)])
                            if pss == 0:
                                pe_w1 = p.op("tensor", lambda e, c_=c_: e.matmul(
                                    psE_[0:32, 2 * TB:3 * TB], w1aV(c_)[:, 64:96], xxxb,
                                    start=(c_ == 0), stop=(c_ == NCH - 1)), incs=[(sPE, 1)])
                            else:
                                pe_w1 = sPE.n
                        ac_t5 = p.op("scalar", lambda e: e.activation(t5s[0], psE_[0:32, 0:TB], AF.Tanh),
                                     waits=[(sPE, pe_w1)], incs=[(sAC, 1)])
                        ac_t5 = p.op("scalar", lambda e: e.activation(t5s[1], psE_[32:64, 0:TB], AF.Tanh),
                                     incs=[(sAC, 1)])
                        if pss == 0:
                            ac_t5 = p.op("scalar", lambda e: e.activation(t5s[2], psE_[0:32, 2 * TB:3 * TB],
                                                                          AF.Tanh), incs=[(sAC, 1)])
                        for oc in range(NCH):
                            mb_ = psF_ if (oc % 2 == 0) else psG_
                            for j in range(ns):
                                rhs = t5s[j]
                                w_ = ([(sAC, ac_t5), (sK, w2tot), (sDV, dv_happ)]
                                      if (j == 0 and oc == 0) else [])
                                if j == 0:
                                    w_ = w_ + [(sDV, dv_mix1[oc % 2])]
                                p.op("tensor", lambda e, j=j, oc=oc, rhs=rhs, mb_=mb_: e.matmul(
                                    mb_[:, j * TB:(j + 1) * TB], w2V(j, oc), rhs,
                                    start=True, stop=True), waits=w_, incs=[(sPE, 1)])
                            pe_m = sPE.n
                            tmB = tm5_cols(oc, 0 if pss == 0 else 3, ns).unsqueeze(2).broadcast_to([128, ns, TB])
                            xxB = xxV(oc).unsqueeze(1).broadcast_to([128, ns, TB])
                            hbB = hbV(oc).unsqueeze(1).broadcast_to([128, ns, TB])
                            md = mixed_dst(oc)
                            psF3 = mb_[:, 0:ns * TB].rearrange("p (j n) -> p j n", j=ns)
                            dv_mix1[oc % 2] = p.op("vector", lambda e, tmB=tmB, psF3=psF3, md=md:
                                                   e.tensor_tensor(md, psF3, tmB, ALU.add),
                                                   waits=[(sPE, pe_m)], incs=[(sDV, 1)], dr=False)
                            p.op("vector", lambda e, xxB=xxB, md=md: e.tensor_tensor(
                                md, md, xxB, ALU.mult))
                            dv_mix = p.op("vector", lambda e, hbB=hbB, md=md: e.tensor_tensor(
                                md, md, hbB, ALU.add), incs=[(sDV, 1)])
                        if pss == 0:
                            for c_ in range(NCH):
                                p.op("tensor", lambda e, c_=c_: e.matmul(
                                    psE_[0:64, 2 * TB:3 * TB], tdw1V(c_), mixed(0, c_),
                                    start=(c_ == 0), stop=(c_ == NCH - 1)),
                                    waits=([(sDV, dv_mix)] if c_ == 0 else []), incs=[(sPE, 1)])
                            pe_tw = sPE.n
                            ac_t5p = p.op("scalar", lambda e, tl=tl: e.activation(
                                TW[0:64, tl:tl + TB], psE_[0:64, 2 * TB:3 * TB], AF.Tanh),
                                waits=[(sPE, pe_tw)], incs=[(sAC, 1)])
                            ac_evac = max(ac_evac, ac_t5p)
                        ac_bank = {0: 0, 1: 0}
                        if pss == 0:
                            first_p = True
                            for (slot, wv_, dstf) in ((2, wv0, rF), (1, wv1, kF)):
                                for oc in range(NCH):
                                    bk = oc % 2
                                    psX = psC_ if bk == 0 else psD_
                                    for c_ in range(NCH):
                                        w_ = []
                                        if c_ == 0:
                                            w_ = [(sAC, ac_bank[bk])]
                                            if first_p:
                                                first_p = False
                                                w_ += [(wsem, wtot), (sDV, dv_mix), (sDV, dv_happ)]
                                        p.op("tensor", lambda e, c_=c_, oc=oc, slot=slot, wv_=wv_, psX=psX:
                                             e.matmul(psX[:, 0:TB],
                                                      wv_[:, c_ * 1024 + oc * 128: c_ * 1024 + oc * 128 + 128],
                                                      mixed(slot, c_),
                                                      start=(c_ == 0), stop=(c_ == NCH - 1)),
                                             waits=w_, incs=[(sPE, 1)])
                                    pe_p = sPE.n
                                    ac_bank[bk] = p.op("scalar", lambda e, oc=oc, tl=tl, psX=psX, dstf=dstf:
                                                       e.activation(dstf(oc, tl, TB), psX[:, 0:TB], AF.Copy),
                                                       waits=[(sPE, pe_p)], incs=[(sAC, 1)], dr=(oc == 0))
                            ac_proj = max(ac_bank[0], ac_bank[1])
                        else:
                            first_p = True
                            for (slot, wv_, dstf, actf) in (
                                    (0, wv0, vT, AF.Copy), (1, wv1, gT, AF.Silu)):
                                for tcl in range(2):
                                    tc = tb * 2 + tcl
                                    for oh in range(2):
                                        bk = (tcl * 2 + oh) % 2
                                        psX = psC_ if bk == 0 else psD_
                                        for c_ in range(NCH):
                                            w_ = []
                                            if c_ == 0:
                                                w_ = [(sAC, ac_bank[bk])]
                                                if first_p:
                                                    first_p = False
                                                    w_ += [(wsem, wtot), (sDV, dv_mix), (sDV, dv_happ)]
                                            p.op("tensor",
                                                 lambda e, c_=c_, slot=slot, wv_=wv_, psX=psX,
                                                 tcl=tcl, oh=oh:
                                                 e.matmul(psX[:, 0:512],
                                                          mixed(slot, c_)[:, tcl * 128:(tcl + 1) * 128],
                                                          wv_[:, c_ * 1024 + oh * 512: c_ * 1024 + oh * 512 + 512],
                                                          start=(c_ == 0), stop=(c_ == NCH - 1)),
                                                 waits=w_, incs=[(sPE, 1)])
                                        pe_p = sPE.n
                                        ac_bank[bk] = p.op("scalar",
                                                           lambda e, tc=tc, oh=oh, psX=psX, dstf=dstf, actf=actf:
                                                           e.activation(dstf(tc, oh * 512, 512),
                                                                        psX[:, 0:512], actf),
                                                           waits=[(sPE, pe_p)], incs=[(sAC, 1)],
                                                           dr=(tcl == 0 and oh == 0))
                            ac_proj = max(ac_bank[0], ac_bank[1])
                    c1_wb_free = sPE.n
                    barrier()
                for c_ in range(NCH):
                    ld("gpsimd", WB[:, c_ * 1024:(c_ + 1) * 1024],
                       dram["wo_b"][c_, :, :], sem=sK,
                       waits=[(sPE, c1_wb_free)] if c_ == 0 else [])
                ld("gpsimd", WB[0:64, 12288:13312], dram["tdw2"][:, :], sem=sK)
                ld("gpsimd", lnwB, dram["lnw"][:, :], sem=sK)
                wo_tot = ld("gpsimd", lnbB, dram["lnb"][:, :], sem=sK)
                SCANf = WB[:, 8192:12288].bitcast(F32)
                eV = SCANf[:, 0:1024]
                aV = SCANf[:, 1024:2048]
                refs = CST[:, 77:85]
                pcol = CST[:, 85:93]
                mu16 = CST[:, 93:109]
                rs16 = CST[:, 109:125]
                diagT = CST[:, 125:141]
                expA = A3[:, 0:1024]
                expR = A3[:, 1024:2048]
                ktV = A3[:, 2048:3072]
                krV = A3[:, 3072:4096]
                ktok = A3[:, 0:1024]
                ruk = A3[:, 1024:2048]
                yng = A3[:, 3072:4096]
                y2V = A4[:, 1024:2048]
                sqV = PSC[:, 0:1024]
                diag_fm = A4[:, 0:512].bitcast(BF16)[0:2, :]
                cmb = [A4[:, 512 + i * 64:576 + i * 64].bitcast(BF16) for i in range(4)]
                psFb = psF_[:, :].bitcast(BF16)
                dv_scan = 0
                ac_scan = 0
                dv_yng = 0
                dv_ktok_prev = [0]
                pe_decay_prev = 0
                for f_ in range(B_C2OFF, B_C2OFF + B_C2N):
                    for q_ in range(2):
                        p.op("tensor", lambda e, f_=f_, q_=q_: e.matmul(
                            psF_[:, q_ * 512:(q_ + 1) * 512], tdw2V(f_),
                            TW[0:64, q_ * 512:(q_ + 1) * 512], start=True, stop=True),
                            waits=([(sDV, dv_yng), (sAC, ac_scan), (sK, wo_tot)] if q_ == 0 else []),
                            incs=[(sPE, 1)])
                    pe_w = sPE.n
                    ac_e = p.op("scalar", lambda e, f_=f_: e.activation(
                        eV, psF_[:, 0:1024], AF.Exp, bias=td_col(f_)),
                        waits=[(sPE, pe_w)], incs=[(sAC, 1)])
                    dv1 = p.op("vector", lambda e: e.tensor_tensor_scan(
                        aV, eV, eV, 0.0, ALU.add, ALU.bypass),
                        waits=[(sAC, ac_e)], incs=[(sDV, 1)])
                    p.op("vector", lambda e: e.memset(refs[:, 0:1], 0.0))
                    p.op("vector", lambda e: e.tensor_copy(refs[:, 1:8], aV[:, 127:1024 - 128:128]))
                    p.op("vector", lambda e: e.tensor_copy(pcol, aV[:, 127::128]))
                    p.op("vector", lambda e: e.tensor_tensor(pcol, refs, pcol, ALU.subtract))
                    a3d = aV.rearrange("p (s n) -> p s n", s=8)
                    dv2 = p.op("vector", lambda e, a3d=a3d: e.tensor_tensor(
                        a3d, a3d, refs.unsqueeze(2).broadcast_to([128, 8, 128]), ALU.subtract),
                        incs=[(sDV, 1)])
                    ac_p = p.op("scalar", lambda e: e.activation(pcol, pcol, AF.Exp),
                                waits=[(sDV, dv2)], incs=[(sAC, 1)])
                    dv3 = p.op("vector", lambda e: e.tensor_tensor(eV, aV, eV, ALU.subtract),
                               incs=[(sDV, 1)])
                    ac_eA = p.op("scalar", lambda e: e.activation(expA, aV, AF.Exp),
                                 waits=[(sDV, dv3)], incs=[(sAC, 1)])
                    ac_eR = p.op("scalar", lambda e: e.activation(expR, eV, AF.Exp, scale=-1.0),
                                 incs=[(sAC, 1)])
                    kslice = A1[:, 8192 + f_ * 1024: 8192 + (f_ + 1) * 1024]
                    rslice = A1[:, f_ * 1024:(f_ + 1) * 1024]
                    dv_kt = p.op("vector", lambda e, kslice=kslice: e.tensor_tensor(
                        ktV, kslice, expA, ALU.mult), waits=[(sAC, ac_eA)], incs=[(sDV, 1)])
                    dv_rt = p.op("vector", lambda e, rslice=rslice: e.tensor_tensor(
                        krV, rslice, expR, ALU.mult), waits=[(sAC, ac_eR)], incs=[(sDV, 1)])
                    dv_ruk = p.op("vector", lambda e, rslice=rslice, kslice=kslice, f_=f_:
                                  e.scalar_tensor_tensor(ruk, rslice, u_col(f_), kslice,
                                                         ALU.mult, ALU.mult),
                                  waits=[(sDV, dv_rt)], incs=[(sDV, 1)])
                    pe_dg = 0
                    dv_dg = 0
                    for q_ in range(2):
                        pe_dg = p.op("tensor", lambda e, q_=q_: e.matmul(
                            psC_[0:2, 0:512], ones2V, ruk[:, q_ * 512:(q_ + 1) * 512],
                            start=True, stop=True),
                            waits=[(sDV, dv_ruk), (sDV, dv_dg)] + ([(sDV, dv_scan)] if q_ == 0 else []),
                            incs=[(sPE, 1)])
                        dv_dg = p.op("vector", lambda e, q_=q_: e.tensor_copy(
                            diag_fm[:, q_ * 512:(q_ + 1) * 512], psC_[0:2, 0:512]),
                            waits=[(sPE, pe_dg)], incs=[(sDV, 1)])
                    for tc in range(8 if 'diagT' not in B_SKIP else 0):
                        pe_dg = p.op("tensor", lambda e, tc=tc: e.transpose(
                            psFb[:, 1024 + tc * 2:1024 + (tc + 1) * 2],
                            diag_fm[:, tc * 128:(tc + 1) * 128],
                            identV[0:2, 0:2]),
                            waits=([(sDV, dv_dg)] if tc == 0 else []), incs=[(sPE, 1)])
                    if 'diagT' not in B_SKIP:
                        dv_dgT = p.op("vector", lambda e: e.tensor_copy(diagT, psFb[:, 1024:1040]),
                                      waits=[(sPE, pe_dg)], incs=[(sDV, 1)])
                    else:
                        dv_dgT = 0
                    pe_t = 0
                    for tc in range(8 if 'ktokT' not in B_SKIP else 0):
                        pe_t = p.op("tensor", lambda e, tc=tc: e.transpose(
                            psFb[:, tc * 128:(tc + 1) * 128], ktV[:, tc * 128:(tc + 1) * 128],
                            identV),
                            waits=([(sDV, dv_kt), (sAC, ac_e), (sDV, dv_ktok_prev[0])] if tc == 0 else []),
                            incs=[(sPE, 1)])
                    dv_ktok = p.op("vector", lambda e: e.tensor_copy(ktok, psFb[:, 0:1024]),
                                   waits=[(sPE, pe_t)], incs=[(sDV, 1)])
                    dv_ktok_prev[0] = dv_ktok
                    dv_cm = {0: 0, 1: 0}
                    dv_supd = dv_state[f_]
                    dv_s1 = dv_state[f_]
                    pe_y_final = 0
                    for tc in range(B_TCN):
                        tcc = tc * 128
                        par = tc % 2
                        pe_cp = 0
                        cbanks = ((psA_[:, 0:128], psB_[:, 0:128]) if par == 0 else
                                  (psF_[:, 0:128], psF_[:, 512:640]))
                        for h2 in range(2):
                            ro = h2 * 64
                            pe_cp = p.op("tensor", lambda e, ro=ro, tcc=tcc, cb=cbanks[h2]: e.matmul(
                                cb, ktV[ro:ro + 64, tcc:tcc + 128],
                                krV[ro:ro + 64, tcc:tcc + 128],
                                start=True, stop=True, tile_position=(ro, 0)),
                                waits=([(sDV, dv_cm[par]), (sDV, dv_rt)] if h2 == 0 else []),
                                incs=[(sPE, 1)])
                        for h2 in range(2):
                            cm = cmb[par * 2 + h2]
                            dv_cm[par] = p.op("vector", lambda e, cm=cm, cb=cbanks[h2]: e.tensor_tensor(
                                cm, cb, slmaskV, ALU.mult),
                                waits=[(sPE, pe_cp)], incs=[(sDV, 1)], dr=(h2 == 0))
                        for h2 in range(2):
                            cm = cmb[par * 2 + h2]
                            vsl = vT(tc, f_ * 128 + h2 * 64, 64)
                            p.op("tensor", lambda e, cm=cm, vsl=vsl, tcc=tcc, h2=h2: e.matmul(
                                psE_[:, tcc + h2 * 64: tcc + h2 * 64 + 64], cm, vsl,
                                start=True, stop=False),
                                waits=[(sDV, dv_cm[par])] + ([(sDV, dv_yng)] if (tc == 0 and h2 == 0) else []),
                                incs=[(sPE, 1)])
                            p.op("tensor", lambda e, tcc=tcc, h2=h2, f_=f_: e.matmul(
                                psE_[:, tcc + h2 * 64: tcc + h2 * 64 + 64],
                                krV[h2 * 64:(h2 + 1) * 64, tcc:tcc + 128],
                                SSTB[h2 * 64:(h2 + 1) * 64, f_ * 64:(f_ + 1) * 64],
                                start=False, stop=True),
                                waits=[(sDV, dv_supd)], incs=[(sPE, 1)])
                        pe_y_final = sPE.n
                        for h2 in range(2):
                            p.op("tensor", lambda e, tc=tc, h2=h2, f_=f_, tcc=tcc: e.matmul(
                                psC_[h2 * 64:(h2 + 1) * 64, 0:64],
                                ktok[:, tcc + h2 * 64: tcc + h2 * 64 + 64],
                                vT(tc, f_ * 128 + h2 * 64, 64),
                                start=True, stop=True, tile_position=(0, h2 * 64)),
                                waits=([(sDV, dv_s1), (sDV, dv_ktok)] if h2 == 0 else []),
                                incs=[(sPE, 1)])
                        pe_kv = sPE.n
                        sw = SST[:, f_ * 64:(f_ + 1) * 64]
                        pp = pcol[:, tc - 1:tc] if tc > 0 else CST[:, 152 + f_:153 + f_]
                        dv_s1 = p.op("vector", lambda e, sw=sw, pp=pp: e.scalar_tensor_tensor(
                            sw, sw, pp, psC_[:, 0:64], ALU.mult, ALU.add),
                            waits=[(sPE, pe_kv)] + ([(sAC, ac_p)] if tc == 0 else []),
                            incs=[(sDV, 1)])
                        dv_supd = p.op("vector", lambda e, sw=sw, f_=f_, tc=tc: e.tensor_scalar(
                            SSTB[:, f_ * 64:(f_ + 1) * 64], sw, pcol[:, tc:tc + 1], None,
                            ALU.mult), incs=[(sDV, 1)])
                        dv_state[f_] = dv_supd
                    p.op("vector", lambda e, f_=f_: e.tensor_copy(
                        CST[:, 152 + f_:153 + f_], pcol[:, 7:8]))
                    vfull = A2[:, 0:8192].rearrange("p (tc n) -> p tc n", tc=8)[:, :, f_ * 128:(f_ + 1) * 128]
                    dgB = diagT.rearrange("p (tc h) -> p tc h", tc=8).unsqueeze(3).broadcast_to([128, 8, 2, 64])
                    p.op("vector", lambda e, vfull=vfull, dgB=dgB: e.tensor_tensor(
                        sqV.bitcast(BF16)[:, 0:1024].rearrange("p (tc h n) -> p tc h n", tc=8, h=2),
                        vfull.rearrange("p tc (h n) -> p tc h n", h=2), dgB, ALU.mult),
                        waits=[(sDV, dv_dgT), (sAC, ac_proj)])
                    dv_y2 = p.op("vector", lambda e: e.tensor_tensor(
                        y2V, psE_[:, 0:1024], sqV.bitcast(BF16)[:, 0:1024], ALU.add),
                        waits=[(sPE, pe_y_final)], incs=[(sDV, 1)])
                    p.op("vector", lambda e: e.tensor_reduce(
                        mu16, y2V.rearrange("p (g n) -> p g n", g=16), AX.X, ALU.add))
                    ac_sq2 = p.op("scalar", lambda e: e.activation(sqV, y2V, AF.Square),
                                  waits=[(sDV, dv_y2)], incs=[(sAC, 1)])
                    ac_scan = ac_sq2
                    p.op("vector", lambda e: e.tensor_reduce(
                        rs16, sqV.rearrange("p (g n) -> p g n", g=16), AX.X, ALU.add),
                        waits=[(sAC, ac_sq2)])
                    p.op("vector", lambda e: e.tensor_scalar_mul(mu16, mu16, 1.0 / 64))
                    p.op("vector", lambda e: e.tensor_scalar_mul(rs16, rs16, 1.0 / 64))
                    p.op("vector", lambda e: e.tensor_tensor(sqV[:, 16:32], mu16, mu16, ALU.mult))
                    dv_v16 = p.op("vector", lambda e: e.tensor_tensor(
                        rs16, rs16, sqV[:, 16:32], ALU.subtract), incs=[(sDV, 1)])
                    p.op("scalar", lambda e: e.activation(rs16, rs16, AF.Ln, bias=c_epsgn),
                         waits=[(sDV, dv_v16)])
                    ac_rs = p.op("scalar", lambda e: e.activation(rs16, rs16, AF.Exp, scale=-0.5),
                                 incs=[(sAC, 1)])
                    muB = mu16[:, 0:12].unsqueeze(2).broadcast_to([128, 12, 64])
                    rsB = rs16[:, 0:12].unsqueeze(2).broadcast_to([128, 12, 64])
                    muH = mu16[:, 12:16].unsqueeze(2).broadcast_to([128, 4, 64])
                    rsH = rs16[:, 12:16].unsqueeze(2).broadcast_to([128, 4, 64])
                    y2g = y2V.rearrange("p (g n) -> p g n", g=16)[:, 0:12, :]
                    y2h = y2V.rearrange("p (g n) -> p g n", g=16)[:, 12:16, :]
                    lnwS = lnwB[:, f_ * 128:(f_ + 1) * 128].unsqueeze(1).broadcast_to([128, 6, 128])
                    lnbS = lnbB[:, f_ * 128:(f_ + 1) * 128].unsqueeze(1).broadcast_to([128, 6, 128])
                    lnwH = lnwB[:, f_ * 128:(f_ + 1) * 128].unsqueeze(1).broadcast_to([128, 2, 128])
                    lnbH = lnbB[:, f_ * 128:(f_ + 1) * 128].unsqueeze(1).broadcast_to([128, 2, 128])
                    y2t = y2V.rearrange("p (tc n) -> p tc n", tc=8)
                    y2tL = y2t[:, 0:6, :]
                    y2tH = y2t[:, 6:8, :]
                    p.op("gpsimd", lambda e, muH=muH, y2h=y2h: e.tensor_tensor(
                        y2h, y2h, muH, ALU.subtract),
                        waits=[(sDV, dv_v16), (sAC, ac_rs), (sK, wo_tot)])
                    p.op("gpsimd", lambda e, rsH=rsH, y2h=y2h: e.tensor_tensor(
                        y2h, y2h, rsH, ALU.mult))
                    p.op("gpsimd", lambda e, lnwH=lnwH, y2tH=y2tH: e.tensor_tensor(
                        y2tH, y2tH, lnwH, ALU.mult))
                    gp_gn = p.op("gpsimd", lambda e, lnbH=lnbH, y2tH=y2tH: e.tensor_tensor(
                        y2tH, y2tH, lnbH, ALU.add), incs=[(sGP, 1)])
                    p.op("vector", lambda e, muB=muB, y2g=y2g: e.tensor_tensor(
                        y2g, y2g, muB, ALU.subtract), waits=[(sAC, ac_rs)])
                    p.op("vector", lambda e, rsB=rsB, y2g=y2g: e.tensor_tensor(
                        y2g, y2g, rsB, ALU.mult))
                    p.op("vector", lambda e, lnwS=lnwS, y2tL=y2tL: e.tensor_tensor(
                        y2tL, y2tL, lnwS, ALU.mult), waits=[(sK, wo_tot)])
                    p.op("vector", lambda e, lnbS=lnbS, y2tL=y2tL: e.tensor_tensor(
                        y2tL, y2tL, lnbS, ALU.add))
                    gfull = A2[:, 8192:16384].rearrange("p (tc n) -> p tc n", tc=8)[:, :, f_ * 128:(f_ + 1) * 128]
                    dv_yngv = p.op("vector", lambda e, gfull=gfull, y2t=y2t: e.tensor_tensor(
                        yng.rearrange("p (tc n) -> p tc n", tc=8), y2t, gfull, ALU.mult),
                        waits=[(sGP, gp_gn)], incs=[(sDV, 1)])
                    pe_t = 0
                    for tc in range(8 if 'yngT' not in B_SKIP else 0):
                        pe_t = p.op("tensor", lambda e, tc=tc: e.transpose(
                            psFb[:, 1024 + tc * 128:1024 + (tc + 1) * 128],
                            yng[:, tc * 128:(tc + 1) * 128], identV),
                            waits=([(sDV, dv_yngv)] if tc == 0 else []),
                            incs=[(sPE, 1)])
                    dv_yng2 = p.op("vector", lambda e, f_=f_: e.tensor_copy(
                        A1[:, 8192 + f_ * 1024: 8192 + (f_ + 1) * 1024], psFb[:, 1024:2048]),
                        waits=[(sPE, pe_t)], incs=[(sDV, 1)])
                    dv_yng = dv_yng2
                    dv_scan = dv_yng2
                barrier()
                dv_bank = {0: 0, 1: 0}
                for oc in range(NCH):
                    for tbb in range(2):
                        bk = (oc * 2 + tbb) % 2
                        ps = psE_ if bk == 0 else psF_
                        for c_ in range(NCH):
                            w_ = [(sK, wo_tot)] if (oc == 0 and tbb == 0 and c_ == 0) else []
                            if c_ == 0:
                                w_ = w_ + [(sDV, dv_bank[bk])]
                            p.op("tensor", lambda e, c_=c_, oc=oc, tbb=tbb, ps=ps:
                                 e.matmul(ps[:, 0:512],
                                          WB[:, c_ * 1024 + oc * 128: c_ * 1024 + oc * 128 + 128],
                                          A1[:, 8192 + c_ * 1024 + tbb * 512: 8192 + c_ * 1024 + tbb * 512 + 512],
                                          start=(c_ == 0), stop=(c_ == NCH - 1)),
                                 waits=w_, incs=[(sPE, 1)])
                        pe1 = sPE.n
                        dv_bank[bk] = p.op("vector", lambda e, oc=oc, tbb=tbb, ps=ps, h0=h0:
                                           e.tensor_tensor(xa(b_, oc)[:, h0 + tbb * 512: h0 + (tbb + 1) * 512],
                                                           xa(b_, oc)[:, h0 + tbb * 512: h0 + (tbb + 1) * 512],
                                                           ps[:, 0:512], ALU.add),
                                           waits=[(sPE, pe1)], incs=[(sDV, 1)])
                barrier()

        for b_ in range(BPC):
            if b_ > 0:
                p.wait("sync", [(sST, sST.n)])
            for c_ in range(NCH):
                x_loads[(b_, c_)] = ld("sync", xa(b_, c_), dram["xF"][b_, c_, :, :])
            x_ready[b_] = sL.n
            attn_a(b_)
            if STAGES >= 2:
                ffn(b_)
            if STAGES >= 3:
                attn_b(b_)
            if STAGES >= 4:
                ffn(b_)
            for c_ in range(NCH):
                p.op("gpsimd", lambda e, b_=b_, c_=c_:
                     e.dma_start(out=yout[b_, c_, :, :], in_=xa(b_, c_)),
                     incs=[(sST, 16)])
        p.wait("gpsimd", [(sST, sST.n)])

        with nc.Block() as block:
            p.finalize(block)

    return nc


_NC_CACHE = {}


def kernel(**inputs):
    host = build_host_inputs(inputs)
    if "nc" not in _NC_CACHE:
        _NC_CACHE["nc"] = build_nc()
    nc = _NC_CACHE["nc"]
    in_maps = []
    for core in range(NCORE):
        m = dict(host)
        m["xF"] = np.ascontiguousarray(host["xF"][core * BPC:(core + 1) * BPC])
        in_maps.append(m)
    res = run_bass_kernel_spmd(nc, in_maps, core_ids=list(range(NCORE)))
    outs = []
    for core in range(NCORE):
        y = np.asarray(res.results[core]["yout"])
        outs.append(y.reshape(BPC, C, T).transpose(0, 2, 1))
    return np.ascontiguousarray(np.concatenate(outs, axis=0)).astype(np.float32)
